# version 02 (final)
# baseline (speedup 1.0000x reference)
import numpy as np
from contextlib import ExitStack
import concourse.bass as bass
import concourse.mybir as mybir
from concourse.bass_utils import run_bass_kernel_spmd

F32 = mybir.dt.float32
BF16 = mybir.dt.bfloat16
AF = mybir.ActivationFunctionType
OP = mybir.AluOpType
AX = mybir.AxisListType

ENGINES = ("sp", "pe", "dve", "act", "pool")
_ATTR = {"sp": "sync", "pe": "tensor", "dve": "vector", "act": "scalar", "pool": "gpsimd"}


class Ctx:

    def __init__(self, nc, stack):
        self.nc = nc
        self.stack = stack
        self.sems = {e: stack.enter_context(nc.semaphore(name=f"sem_{e}")) for e in ENGINES}
        self.cnt = {e: 0 for e in ENGINES}
        self.steps = []
        self.writer = {}
        self.readers = {}
        self.waited = {e: {p: 0 for p in ENGINES} for e in ENGINES}
        self._names = {}

    def emit(self, eng, fn, reads=(), writes=(), dma=False):
        inc = 16 if dma else 1
        waits = {}

        def need(p, v):
            if p == eng == "pe":
                return
            if v > self.waited[eng][p] and not (p == eng and False):
                waits[p] = max(waits.get(p, 0), v)

        for k in reads:
            w = self.writer.get(id(k))
            if w:
                need(*w)
        for k in writes:
            w = self.writer.get(id(k))
            if w:
                need(*w)
            for p, v in self.readers.get(id(k), {}).items():
                need(p, v)
        for p, v in waits.items():
            self.waited[eng][p] = v
        my_after = self.cnt[eng] + inc
        for k in writes:
            self.writer[id(k)] = (eng, my_after)
            self.readers[id(k)] = {}
            self._names[id(k)] = getattr(k, "name", "?")
        for k in reads:
            self.readers.setdefault(id(k), {})[eng] = my_after
        self.cnt[eng] = my_after
        self.steps.append((eng, sorted(waits.items()), fn, inc))

    def finalize(self):
        nc = self.nc
        with nc.Block() as block:
            for ename in ENGINES:
                def body(engine, ename=ename):
                    sem = self.sems[ename]
                    for (e, waits, fn, inc) in self.steps:
                        if e != ename:
                            continue
                        for (p, v) in waits:
                            engine.wait_ge(self.sems[p], v)
                        fn(engine).then_inc(sem, inc)
                getattr(block, _ATTR[ename])(body)

    def sb(self, name, p, f, dt=F32):
        t = self.stack.enter_context(self.nc.sbuf_tensor(name, [p, f], dt))
        return t

    def ps(self, name, p=128, f=512, dt=F32):
        return self.stack.enter_context(self.nc.psum_tensor(name, [p, f], dt))

    def dma(self, dst, src, reads=(), writes=(), q="sp"):
        self.emit(q, lambda e: e.dma_start(out=dst, in_=src), reads=reads,
                  writes=writes, dma=True)

    def mm(self, out, lhsT, rhs, start, stop, reads=(), writes=()):
        self.emit("pe", lambda e: e.matmul(out, lhsT, rhs, start=start, stop=stop),
                  reads=reads, writes=writes)

    def tr(self, out, in_, ident, reads=(), writes=()):
        self.emit("pe", lambda e: e.transpose(out, in_, ident),
                  reads=reads, writes=writes)

    def tt(self, out, a, b, op, reads=(), writes=(), eng="dve"):
        self.emit(eng, lambda e: e.tensor_tensor(out, a, b, op), reads=reads, writes=writes)

    def ts(self, out, a, s1, op0, s2=None, op1=OP.bypass, reads=(), writes=(), eng="dve"):
        self.emit(eng, lambda e: e.tensor_scalar(out, a, s1, s2, op0, op1),
                  reads=reads, writes=writes)

    def stt(self, out, in0, scalar, in1, op0, op1, reads=(), writes=(), eng="dve"):
        self.emit(eng, lambda e: e.scalar_tensor_tensor(out, in0, scalar, in1, op0, op1),
                  reads=reads, writes=writes)

    def red(self, out, in_, op, reads=(), writes=(), eng="dve"):
        self.emit(eng, lambda e: e.tensor_reduce(out, in_, AX.X, op), reads=reads, writes=writes)

    def act(self, out, in_, func, bias=0.0, scale=1.0, reads=(), writes=()):
        self.emit("act", lambda e: e.activation(out, in_, func, bias=bias, scale=scale),
                  reads=reads, writes=writes)

    def cp(self, out, in_, reads=(), writes=(), eng="act"):
        if eng == "act":
            self.emit(eng, lambda e: e.copy(out, in_), reads=reads, writes=writes)
        else:
            self.emit(eng, lambda e: e.tensor_copy(out, in_), reads=reads, writes=writes)

    def rec(self, out, in_, reads=(), writes=()):
        self.emit("dve", lambda e: e.reciprocal(out, in_), reads=reads, writes=writes)

    def mset(self, ap, v, reads=(), writes=(), eng="dve"):
        self.emit(eng, lambda e: e.memset(ap, v), reads=reads, writes=writes)

    def final_wait(self, eng):
        val = self.cnt[eng]
        self.steps.append((eng, [(eng, val)], lambda e: e.engine_nop(), 1))
        self.cnt[eng] += 1


def host_consts():
    I = np.eye(128, dtype=np.float32)
    s = np.arange(128)[:, None]
    t = np.arange(128)[None, :]
    TRIU = (s <= t).astype(np.float32)
    ALL1 = np.ones((128, 128), np.float32)
    IOTA = np.broadcast_to(np.arange(64, dtype=np.float32), (128, 64)).copy()
    G64 = np.zeros((128, 8), np.float32)
    for c in range(8):
        G64[c * 8:(c + 1) * 8, c] = 1.0
    return np.concatenate([I, TRIU, ALL1, IOTA, G64], axis=1)


def build_block0():
    nc = bass.Bass("TRN2", target_bir_lowering=False)
    T, D, H = 2048, 1024, 64
    NT, KC = T // 128, D // 128
    x_h = nc.dram_tensor("x", [T, D], F32, kind="ExternalInput").ap()
    wqkv_h = nc.dram_tensor("wqkv", [D, 192], F32, kind="ExternalInput").ap()
    wo_h = nc.dram_tensor("wo", [H, D], F32, kind="ExternalInput").ap()
    sig_h = nc.dram_tensor("sig", [64, D], F32, kind="ExternalInput").ap()
    knots_h = nc.dram_tensor("knots", [8], F32, kind="ExternalInput").ap()
    temp_h = nc.dram_tensor("temp", [1], F32, kind="ExternalInput").ap()
    cw1_h = nc.dram_tensor("cw1", [D, 256], F32, kind="ExternalInput").ap()
    cb1_h = nc.dram_tensor("cb1", [256], F32, kind="ExternalInput").ap()
    cw2_h = nc.dram_tensor("cw2", [256, 2], F32, kind="ExternalInput").ap()
    cb2_h = nc.dram_tensor("cb2", [2], F32, kind="ExternalInput").ap()
    spl_h = nc.dram_tensor("spl", [64, 65], F32, kind="ExternalInput").ap()
    dirs_h = nc.dram_tensor("dirs", [64, D], F32, kind="ExternalInput").ap()
    gam_h = nc.dram_tensor("gamma", [D], F32, kind="ExternalInput").ap()
    bet_h = nc.dram_tensor("bet", [D], F32, kind="ExternalInput").ap()
    osc_h = nc.dram_tensor("osc", [1], F32, kind="ExternalInput").ap()
    cn_h = nc.dram_tensor("CNH", [128, 456], F32, kind="ExternalInput").ap()
    x1_h = nc.dram_tensor("x1", [T, D], F32, kind="ExternalOutput").ap()

    with ExitStack() as stack:
        c = Ctx(nc, stack)
        CN = c.sb("CN", 128, 456)
        I128 = CN[:, 0:128]
        TRIU = CN[:, 128:256]
        ALL1 = CN[:, 256:384]
        IOTA = CN[:, 384:448]
        G64 = CN[0:64, 448:456]
        c.dma(CN[:, :], cn_h, writes=[CN])
        wqkv = [c.sb(f"wqkv{k}", 128, 192) for k in range(KC)]
        for k in range(KC):
            c.dma(wqkv[k][:, :], wqkv_h[k * 128:(k + 1) * 128, :], writes=[wqkv[k]])
        wo = c.sb("s_wo", 64, D)
        c.dma(wo[:, :], wo_h, writes=[wo])
        sig = c.sb("s_sig", 64, D)
        c.dma(sig[:, :], sig_h, writes=[sig])
        cw1 = [c.sb(f"cw1_{k}", 128, 256) for k in range(KC)]
        for k in range(KC):
            c.dma(cw1[k][:, :], cw1_h[k * 128:(k + 1) * 128, :], writes=[cw1[k]])
        cw2 = c.sb("s_cw2", 128, 4)
        c.dma(cw2[:, 0:2], cw2_h[0:128, :], writes=[cw2])
        c.dma(cw2[:, 2:4], cw2_h[128:256, :], writes=[cw2])
        spl = c.sb("s_spl", 64, 65)
        c.dma(spl[:, :], spl_h, writes=[spl])
        dirs = c.sb("s_dirs", 64, D)
        c.dma(dirs[:, :], dirs_h, writes=[dirs])
        rows = c.sb("rows", 1, 256 + 2 + 8 + 1 + 1 + 1024 + 1024)
        c.dma(rows[:, 0:256], cb1_h.rearrange("(o f) -> o f", o=1), writes=[rows])
        c.dma(rows[:, 256:258], cb2_h.rearrange("(o f) -> o f", o=1), writes=[rows])
        c.dma(rows[:, 258:266], knots_h.rearrange("(o f) -> o f", o=1), writes=[rows])
        c.dma(rows[:, 266:267], temp_h.rearrange("(o f) -> o f", o=1), writes=[rows])
        c.dma(rows[:, 267:268], osc_h.rearrange("(o f) -> o f", o=1), writes=[rows])
        c.dma(rows[:, 268:1292], gam_h.rearrange("(o f) -> o f", o=1), writes=[rows])
        c.dma(rows[:, 1292:2316], bet_h.rearrange("(o f) -> o f", o=1), writes=[rows])
        cb1r = rows[:, 0:256]
        cb2r = rows[:, 256:258]
        P = [c.ps(f"pb{i}") for i in range(8)]

        drv = c.sb("drv", 1, 16)
        c.emit("dve", lambda e: e.tensor_scalar(drv[:, 0:1], rows[:, 266:267], 1e-6, None, OP.add),
               reads=[rows], writes=[drv])
        c.rec(drv[:, 0:1], drv[:, 0:1], reads=[drv], writes=[drv])
        c.tt(drv[:, 1:8], rows[:, 259:266], rows[:, 258:265], OP.subtract, reads=[rows], writes=[drv])
        c.cp(drv[:, 8:9], rows[:, 258:259], reads=[rows], writes=[drv])
        c.cp(drv[:, 9:10], rows[:, 267:268], reads=[rows], writes=[drv])
        c.mm(P[0][:, 0:16], ALL1[0:1, :], drv[0:1, 0:16], True, True, reads=[CN, drv], writes=[P[0]])
        bc = c.sb("bc", 128, 16)
        c.cp(bc[:, :], P[0][:, 0:16], reads=[P[0]], writes=[bc])
        RT, K0, OSC = bc[:, 0:1], bc[:, 8:9], bc[:, 9:10]
        GREP = c.sb("GREP", 128, D)
        BREP = c.sb("BREP", 128, D)
        for g in range(2):
            c.mm(P[1][:, :], ALL1[0:1, :], rows[0:1, 268 + 512 * g:268 + 512 * (g + 1)],
                 True, True, reads=[CN, rows], writes=[P[1]])
            c.cp(GREP[:, 512 * g:512 * (g + 1)], P[1][:, :], reads=[P[1]], writes=[GREP])
            c.mm(P[2][:, :], ALL1[0:1, :], rows[0:1, 1292 + 512 * g:1292 + 512 * (g + 1)],
                 True, True, reads=[CN, rows], writes=[P[2]])
            c.cp(BREP[:, 512 * g:512 * (g + 1)], P[2][:, :], reads=[P[2]], writes=[BREP])
        idxm8 = c.sb("idxm8", 128, 8)
        c.ts(idxm8[:, :], IOTA[:, 0:8], 8.0, OP.subtract, reads=[CN], writes=[idxm8])
        qs = c.sb("qs", 64, D)
        sc_a = c.sb("sc_a", 64, D)
        c.ts(sc_a[:, :], sig[:, :], 0.3, OP.is_gt, reads=[sig], writes=[sc_a])
        c.ts(qs[:, :], sig[:, :], -0.3, OP.is_lt, reads=[sig], writes=[qs])
        c.tt(qs[:, :], sc_a[:, :], qs[:, :], OP.subtract, reads=[sc_a, qs], writes=[qs])
        cs = c.sb("cs", 8, D)
        for g in range(2):
            c.mm(P[3][0:8, :], G64, qs[:, 512 * g:512 * (g + 1)], True, True,
                 reads=[CN, qs], writes=[P[3]])
            c.ts(sc_a[0:8, 0:512], P[3][0:8, :], 0.0, OP.is_gt, reads=[P[3]], writes=[sc_a])
            c.ts(cs[:, 512 * g:512 * (g + 1)], P[3][0:8, :], 0.0, OP.is_lt, reads=[P[3]], writes=[cs])
            c.tt(cs[:, 512 * g:512 * (g + 1)], sc_a[0:8, 0:512], cs[:, 512 * g:512 * (g + 1)],
                 OP.subtract, reads=[sc_a, cs], writes=[cs])
        sigT = c.sb("sigT", 128, 512)
        csT = c.sb("csT", 128, 64)
        for k in range(KC):
            c.tr(P[4][:, 0:64], qs[:, k * 128:(k + 1) * 128], I128[0:64, 0:64], reads=[qs, CN], writes=[P[4]])
            c.cp(sigT[:, k * 64:(k + 1) * 64], P[4][:, 0:64], reads=[P[4]], writes=[sigT])
            c.tr(P[5][:, 0:8], cs[:, k * 128:(k + 1) * 128], I128[0:8, 0:8], reads=[cs, CN], writes=[P[5]])
            c.cp(csT[:, k * 8:(k + 1) * 8], P[5][:, 0:8], reads=[P[5]], writes=[csT])

        NR = 3
        x_t = [c.sb(f"x_t{r}", 128, D) for r in range(NR)]
        xn_t = [c.sb(f"xn_t{r}", 128, D) for r in range(NR)]
        xnT = [[c.sb(f"xnT{r}_{k}", 128, 128) for k in range(KC)] for r in range(2)]
        s1 = [c.sb(f"s1_{r}", 128, D) for r in range(2)]
        qkv = [c.sb(f"qkv{r}", 128, 192) for r in range(2)]
        qT_s = [c.sb(f"qT_s{r}", 64, 128) for r in range(2)]
        vT_s = [c.sb(f"vT_s{r}", 64, 128) for r in range(2)]
        A_s = [c.sb(f"A_s{r}", 128, 128) for r in range(2)]
        sc128 = [c.sb(f"sc128_{r}", 128, 128) for r in range(4)]
        cols = [c.sb(f"cols{r}", 128, 8) for r in range(4)]
        SM = c.sb("SM", 64, 64)
        zrow = c.sb("zrow", 1, 64)
        o_s = [c.sb(f"o_s{r}", 128, 64) for r in range(2)]
        att = [c.sb(f"att{r}", 128, D) for r in range(2)]
        ffn = [c.sb(f"ffn{r}", 128, D) for r in range(2)]
        gath = [c.sb(f"gath{r}", 128, 65) for r in range(2)]
        hcmp = [c.sb(f"hcmp{r}", 128, 256) for r in range(2)]
        c.mset(SM[:, :], 0.0, writes=[SM])
        c.mset(zrow[:, :], 0.0, writes=[zrow])

        for t in range(NT):
            r = t % NR
            r2 = t % 2
            xt, xnt = x_t[r], xn_t[r]
            ts_ = slice(t * 128, (t + 1) * 128)
            c.dma(xt[:, :], x_h[ts_, :], writes=[xt])
            mu = cols[0]
            c.red(mu[:, 0:1], xt[:, :], OP.add, reads=[xt], writes=[mu])
            c.ts(mu[:, 0:1], mu[:, 0:1], 1.0 / D, OP.mult, reads=[mu], writes=[mu])
            c.ts(xnt[:, :], xt[:, :], mu[:, 0:1], OP.subtract, reads=[xt, mu], writes=[xnt])
            sq = s1[r2]
            c.act(sq[:, :], xnt[:, :], AF.Square, reads=[xnt], writes=[sq])
            c.red(mu[:, 1:2], sq[:, :], OP.add, reads=[sq], writes=[mu])
            c.ts(mu[:, 1:2], mu[:, 1:2], 1.0 / D, OP.mult, 1e-5, OP.add, reads=[mu], writes=[mu])
            c.act(mu[:, 2:3], mu[:, 1:2], AF.Sqrt, reads=[mu], writes=[mu])
            c.rec(mu[:, 3:4], mu[:, 2:3], reads=[mu], writes=[mu])
            c.ts(xnt[:, :], xnt[:, :], mu[:, 3:4], OP.mult, reads=[xnt, mu], writes=[xnt])
            for k in range(KC):
                c.tr(P[k % 2][:, 0:128], xnt[:, k * 128:(k + 1) * 128], I128,
                     reads=[xnt, CN], writes=[P[k % 2]])
                c.cp(xnT[r2][k][:, :], P[k % 2][:, 0:128], reads=[P[k % 2]], writes=[xnT[r2][k]])
            XT = xnT[r2]
            for k in range(KC):
                c.mm(P[2][:, 0:192], XT[k][:, :], wqkv[k][:, :], k == 0, k == KC - 1,
                     reads=[XT[k], wqkv[k]], writes=[P[2]])
            qkvt = qkv[r2]
            c.ts(qkvt[:, 0:128], P[2][:, 0:128], 0.0, OP.min, reads=[P[2]], writes=[qkvt])
            c.act(qkvt[:, 0:128], qkvt[:, 0:128], AF.Exp, reads=[qkvt], writes=[qkvt])
            c.ts(sc128[0][:, :], P[2][:, 0:128], 0.0, OP.max, reads=[P[2]], writes=[sc128[0]])
            c.tt(qkvt[:, 0:128], qkvt[:, 0:128], sc128[0][:, :], OP.add,
                 reads=[qkvt, sc128[0]], writes=[qkvt])
            c.cp(qkvt[:, 128:192], P[2][:, 128:192], reads=[P[2]], writes=[qkvt])
            for k in range(KC):
                c.mm(P[3][0:64, 0:128], wqkv[k][:, 0:64], XT[k][:, :], k == 0, k == KC - 1,
                     reads=[wqkv[k], XT[k]], writes=[P[3]])
            for k in range(KC):
                c.mm(P[4][0:64, 0:128], wqkv[k][:, 128:192], XT[k][:, :], k == 0, k == KC - 1,
                     reads=[wqkv[k], XT[k]], writes=[P[4]])
            qvt, vvt = qT_s[r2], vT_s[r2]
            c.ts(qvt[:, :], P[3][0:64, 0:128], 0.0, OP.min, reads=[P[3]], writes=[qvt])
            c.act(qvt[:, :], qvt[:, :], AF.Exp, reads=[qvt], writes=[qvt])
            c.ts(sc128[1][0:64, :], P[3][0:64, 0:128], 0.0, OP.max, reads=[P[3]], writes=[sc128[1]])
            c.tt(qvt[:, :], qvt[:, :], sc128[1][0:64, :], OP.add,
                 reads=[qvt, sc128[1]], writes=[qvt])
            c.cp(vvt[:, :], P[4][0:64, 0:128], reads=[P[4]], writes=[vvt])
            c.mm(P[5][:, 0:128], vvt[:, :], qvt[:, :], True, True,
                 reads=[qvt, vvt], writes=[P[5]])
            c.tt(A_s[r2][:, :], P[5][:, 0:128], TRIU, OP.mult, reads=[P[5], CN], writes=[A_s[r2]])
            c.mm(P[6][:, 0:64], A_s[r2][:, :], qkvt[:, 64:128], True, False,
                 reads=[A_s[r2], qkvt], writes=[P[6]])
            c.mm(P[6][:, 0:64], qvt[:, :], SM[:, :], False, True,
                 reads=[qvt, SM], writes=[P[6]])
            c.mm(P[7][:, 0:64], TRIU, qkvt[:, 64:128], True, False,
                 reads=[CN, qkvt], writes=[P[7]])
            c.mm(P[7][:, 0:64], ALL1[0:1, :], zrow[:, :], False, True,
                 reads=[CN, zrow], writes=[P[7]])
            c.mm(P[0][0:64, 0:64], qkvt[:, 128:192], qkvt[:, 64:128], True, True,
                 reads=[qkvt], writes=[P[0]])
            c.tt(sc128[2][:, 0:64], P[7][:, 0:64], qkvt[:, 0:64], OP.mult,
                 reads=[P[7], qkvt], writes=[sc128[2]])
            c.red(cols[1][:, 0:1], sc128[2][:, 0:64], OP.add, reads=[sc128[2]], writes=[cols[1]])
            c.ts(cols[1][:, 0:1], cols[1][:, 0:1], 1e-6, OP.max, reads=[cols[1]], writes=[cols[1]])
            c.rec(cols[1][:, 1:2], cols[1][:, 0:1], reads=[cols[1]], writes=[cols[1]])
            c.ts(o_s[r2][:, :], P[6][:, 0:64], cols[1][:, 1:2], OP.mult,
                 reads=[P[6], cols[1]], writes=[o_s[r2]])
            c.mm(P[1][0:1, 0:64], ALL1[:, 0:1], qkvt[:, 64:128], True, True,
                 reads=[CN, qkvt], writes=[P[1]])
            c.tt(zrow[:, :], zrow[:, :], P[1][0:1, 0:64], OP.add, reads=[zrow, P[1]], writes=[zrow])
            c.tt(SM[:, :], SM[:, :], P[0][0:64, 0:64], OP.add, reads=[SM, P[0]], writes=[SM])
            c.tr(P[5][0:64, 0:128], o_s[r2][:, :], I128, reads=[o_s[r2], CN], writes=[P[5]])
            c.cp(sc128[3][0:64, :], P[5][0:64, 0:128], reads=[P[5]], writes=[sc128[3]])
            for g in range(2):
                c.mm(P[6][:, :], sc128[3][0:64, :], wo[:, 512 * g:512 * (g + 1)], True, True,
                     reads=[sc128[3], wo], writes=[P[6]])
                c.cp(att[r2][:, 512 * g:512 * (g + 1)], P[6][:, :], reads=[P[6]], writes=[att[r2]])
            for k in range(KC):
                c.mm(P[2][:, 0:64], XT[k][:, :], sigT[:, k * 64:(k + 1) * 64], k == 0, k == KC - 1,
                     reads=[XT[k], sigT], writes=[P[2]])
            for k in range(KC):
                c.mm(P[3][:, 0:8], XT[k][:, :], csT[:, k * 8:(k + 1) * 8], k == 0, k == KC - 1,
                     reads=[XT[k], csT], writes=[P[3]])
            tg = sc128[0]
            cg = cols[2]
            for (src, dst, w_, n) in ((P[2], tg, sc128[1], 64), (P[3], cg, cols[3], 8)):
                c.ts(w_[:, 0:n], src[:, 0:n], RT, OP.mult, reads=[src, bc], writes=[w_])
                c.act(w_[:, 0:n], w_[:, 0:n], AF.Sigmoid, reads=[w_], writes=[w_])
                c.ts(dst[:, 0:n], w_[:, 0:n], 0.0, OP.mult, K0, OP.add, reads=[w_, bc], writes=[dst])
                for j in range(7):
                    c.ts(src[:, 0:n], w_[:, 0:n], 7.0, OP.mult, float(-j), OP.add,
                         reads=[w_], writes=[src])
                    c.ts(src[:, 0:n], src[:, 0:n], 0.0, OP.max, 1.0, OP.min,
                         reads=[src], writes=[src])
                    c.stt(dst[:, 0:n], src[:, 0:n], bc[:, 1 + j:2 + j], dst[:, 0:n],
                          OP.mult, OP.add, reads=[src, bc, dst], writes=[dst])
            cb = cols[3]
            c.red(cb[:, 0:1], cg[:, 0:8], OP.max, reads=[cg], writes=[cb])
            c.ts(cols[0][:, 0:8], cg[:, 0:8], cb[:, 0:1], OP.is_ge, reads=[cg, cb], writes=[cols[0]])
            c.tt(cols[0][:, 0:8], cols[0][:, 0:8], idxm8[:, :], OP.mult,
                 reads=[cols[0], idxm8], writes=[cols[0]])
            c.ts(cols[0][:, 0:8], cols[0][:, 0:8], 8.0, OP.add, reads=[cols[0]], writes=[cols[0]])
            c.red(cb[:, 1:2], cols[0][:, 0:8], OP.min, reads=[cols[0]], writes=[cb])
            c.mset(cb[:, 2:3], 0.0, writes=[cb])
            for cl in range(8):
                sl = slice(cl * 8, cl * 8 + 8)
                c.red(cb[:, 3:4], tg[:, sl], OP.max, reads=[tg], writes=[cb])
                c.ts(cols[0][:, 0:8], tg[:, sl], cb[:, 3:4], OP.is_ge, reads=[tg, cb], writes=[cols[0]])
                c.tt(cols[0][:, 0:8], cols[0][:, 0:8], idxm8[:, :], OP.mult,
                     reads=[cols[0], idxm8], writes=[cols[0]])
                c.ts(cols[0][:, 0:8], cols[0][:, 0:8], 8.0, OP.add, reads=[cols[0]], writes=[cols[0]])
                c.red(cb[:, 4:5], cols[0][:, 0:8], OP.min, reads=[cols[0]], writes=[cb])
                c.ts(cb[:, 4:5], cb[:, 4:5], float(8 * cl), OP.add, reads=[cb], writes=[cb])
                c.ts(cb[:, 5:6], cb[:, 1:2], float(cl), OP.is_equal, reads=[cb], writes=[cb])
                c.stt(cb[:, 2:3], cb[:, 4:5], cb[:, 5:6], cb[:, 2:3], OP.mult, OP.add,
                      reads=[cb], writes=[cb])
            c.ts(tg[:, 0:64], IOTA, cb[:, 2:3], OP.is_equal, reads=[CN, cb], writes=[tg])
            c.tr(P[4][0:64, 0:128], tg[:, 0:64], I128, reads=[tg, CN], writes=[P[4]])
            c.cp(sc128[1][0:64, :], P[4][0:64, 0:128], reads=[P[4]], writes=[sc128[1]])
            c.mm(P[5][:, 0:65], sc128[1][0:64, :], spl[:, :], True, True,
                 reads=[sc128[1], spl], writes=[P[5]])
            gt = gath[r2]
            c.cp(gt[:, :], P[5][:, 0:65], reads=[P[5]], writes=[gt])
            for k in range(KC):
                c.mm(P[6][:, 0:256], XT[k][:, :], cw1[k][:, :], k == 0, False,
                     reads=[XT[k], cw1[k]], writes=[P[6]])
            c.mm(P[6][:, 0:256], ALL1[0:1, :], cb1r, False, True, reads=[CN, rows], writes=[P[6]])
            hc = hcmp[r2]
            c.act(hc[:, :], P[6][:, 0:256], AF.Gelu, reads=[P[6]], writes=[hc])
            for k2 in range(2):
                c.tr(P[7][:, 0:128], hc[:, k2 * 128:(k2 + 1) * 128], I128, reads=[hc, CN], writes=[P[7]])
                c.cp(sc128[1][:, 0:128], P[7][:, 0:128], reads=[P[7]], writes=[sc128[1]])
                c.mm(P[0][:, 0:2], sc128[1][:, 0:128], cw2[:, 2 * k2:2 * k2 + 2], k2 == 0, False,
                     reads=[sc128[1], cw2], writes=[P[0]])
            c.mm(P[0][:, 0:2], ALL1[0:1, :], cb2r, False, True, reads=[CN, rows], writes=[P[0]])
            cmp_ = cb
            c.act(cmp_[:, 6:8], P[0][:, 0:2], AF.Tanh, reads=[P[0]], writes=[cmp_])
            hh = cols[0]
            hh = sc128[1]
            c.ts(hh[:, 0:16], gt[:, 0:16], cmp_[:, 6:7], OP.mult, reads=[gt, cmp_], writes=[hh])
            c.stt(hh[:, 16:32], gt[:, 16:32], cmp_[:, 7:8], hh[:, 0:16], OP.mult, OP.add,
                  reads=[gt, cmp_, hh], writes=[hh])
            c.tt(hh[:, 0:16], hh[:, 16:32], gt[:, 32:48], OP.add, reads=[hh, gt], writes=[hh])
            c.ts(hh[:, 0:16], hh[:, 0:16], 0.0, OP.max, reads=[hh], writes=[hh])
            c.tt(hh[:, 16:32], hh[:, 0:16], gt[:, 48:64], OP.mult, reads=[hh, gt], writes=[hh])
            c.red(cb[:, 3:4], hh[:, 16:32], OP.add, reads=[hh], writes=[cb])
            c.tt(cb[:, 3:4], cb[:, 3:4], gt[:, 64:65], OP.add, reads=[cb, gt], writes=[cb])
            c.ts(tg[:, 0:64], tg[:, 0:64], cb[:, 3:4], OP.mult, reads=[tg, cb], writes=[tg])
            c.tr(P[4][0:64, 0:128], tg[:, 0:64], I128, reads=[tg, CN], writes=[P[4]])
            c.cp(sc128[1][0:64, :], P[4][0:64, 0:128], reads=[P[4]], writes=[sc128[1]])
            ft = ffn[r2]
            for g in range(2):
                c.mm(P[6][:, :], sc128[1][0:64, :], dirs[:, 512 * g:512 * (g + 1)], True, True,
                     reads=[sc128[1], dirs], writes=[P[6]])
                c.stt(ft[:, 512 * g:512 * (g + 1)], P[6][:, :], OSC, xnt[:, 512 * g:512 * (g + 1)],
                      OP.mult, OP.add, reads=[P[6], bc, xnt], writes=[ft])
            mu2 = cols[1]
            c.red(mu2[:, 0:1], ft[:, :], OP.add, reads=[ft], writes=[mu2])
            c.ts(mu2[:, 0:1], mu2[:, 0:1], 1.0 / D, OP.mult, reads=[mu2], writes=[mu2])
            c.ts(ft[:, :], ft[:, :], mu2[:, 0:1], OP.subtract, reads=[ft, mu2], writes=[ft])
            c.act(s1[r2][:, :], ft[:, :], AF.Square, reads=[ft], writes=[s1[r2]])
            c.red(mu2[:, 1:2], s1[r2][:, :], OP.add, reads=[s1[r2]], writes=[mu2])
            c.ts(mu2[:, 1:2], mu2[:, 1:2], 1.0 / D, OP.mult, 1e-5, OP.add, reads=[mu2], writes=[mu2])
            c.act(mu2[:, 2:3], mu2[:, 1:2], AF.Sqrt, reads=[mu2], writes=[mu2])
            c.rec(mu2[:, 3:4], mu2[:, 2:3], reads=[mu2], writes=[mu2])
            c.ts(ft[:, :], ft[:, :], mu2[:, 3:4], OP.mult, reads=[ft, mu2], writes=[ft])
            c.tt(ft[:, :], ft[:, :], GREP[:, :], OP.mult, reads=[ft, GREP], writes=[ft])
            c.tt(ft[:, :], ft[:, :], BREP[:, :], OP.add, reads=[ft, BREP], writes=[ft])
            c.tt(ft[:, :], ft[:, :], att[r2][:, :], OP.add, reads=[ft, att[r2]], writes=[ft])
            c.tt(ft[:, :], ft[:, :], xt[:, :], OP.add, reads=[ft, xt], writes=[ft])
            c.dma(x1_h[ts_, :], ft[:, :], reads=[ft], q="pool")
        c.final_wait("pool")
        c.finalize()
    return nc


def block0_inputs(d):
    cn = host_consts()
    x = np.ascontiguousarray(d["L0_attn_x"])
    com = dict(
        wqkv=np.ascontiguousarray(np.concatenate(
            [d["L0_attn_W_q"], d["L0_attn_W_k"], d["L0_attn_W_v"]], axis=1)),
        wo=np.ascontiguousarray(d["L0_attn_W_o"]),
        sig=np.ascontiguousarray(d["L0_ffn_signatures_raw"]),
        knots=np.ascontiguousarray(d["L0_ffn_knot_values"]),
        temp=np.ascontiguousarray(d["L0_ffn_temperature"]),
        cw1=np.ascontiguousarray(d["L0_ffn_compress_w1"]),
        cb1=np.ascontiguousarray(d["L0_ffn_compress_b1"]),
        cw2=np.ascontiguousarray(d["L0_ffn_compress_w2"]),
        cb2=np.ascontiguousarray(d["L0_ffn_compress_b2"]),
        spl=np.ascontiguousarray(np.concatenate(
            [d["L0_ffn_spline_w1"].reshape(64, 32), d["L0_ffn_spline_b1"],
             d["L0_ffn_spline_w2"].reshape(64, 16), d["L0_ffn_spline_b2"]], axis=1)),
        dirs=np.ascontiguousarray(d["L0_ffn_directions"]),
        gamma=np.ascontiguousarray(d["L0_ffn_norm_gamma"]),
        bet=np.ascontiguousarray(d["L0_ffn_norm_beta"]),
        osc=np.ascontiguousarray(d["L0_ffn_output_scale"]),
        CNH=cn,
    )
    return [dict(com, x=np.ascontiguousarray(x[b])) for b in range(8)]


def host_consts_b():
    s = np.arange(64)[:, None]
    t = np.arange(64)[None, :]
    tu = (s <= t).astype(np.float32)
    BD = np.zeros((128, 128), np.float32)
    BD[0:64, 0:64] = tu
    BD[64:128, 64:128] = tu
    TLBD = BD.T.copy()
    SBD = BD - np.eye(128, dtype=np.float32)
    TSBD = TLBD - np.eye(128, dtype=np.float32)
    TWOI = 2.0 * np.eye(128, dtype=np.float32)
    return np.concatenate([BD, TLBD, SBD, TSBD, TWOI], axis=1)


def build_l1attn():
    nc = bass.Bass("TRN2", target_bir_lowering=False)
    T, D, HN = 2048, 1024, 8
    NT, KC = T // 128, D // 128
    x1_h = nc.dram_tensor("x1", [T, D], F32, kind="ExternalInput").ap()
    wqkvz_h = nc.dram_tensor("wqkvz", [D, 4096], F32, kind="ExternalInput").ap()
    wba_h = nc.dram_tensor("wba", [D, 16], F32, kind="ExternalInput").ap()
    cw_h = nc.dram_tensor("cw", [3072, 4], F32, kind="ExternalInput").ap()
    dtb_h = nc.dram_tensor("dtb", [8], F32, kind="ExternalInput").ap()
    alog_h = nc.dram_tensor("alog", [8], F32, kind="ExternalInput").ap()
    nw_h = nc.dram_tensor("nw", [128], F32, kind="ExternalInput").ap()
    wout_h = nc.dram_tensor("wout", [D, D], F32, kind="ExternalInput").ap()
    cn_h = nc.dram_tensor("CNH", [128, 456], F32, kind="ExternalInput").ap()
    cb_h = nc.dram_tensor("CBH", [128, 640], F32, kind="ExternalInput").ap()
    at_h = nc.dram_tensor("attn1", [T, D], F32, kind="ExternalOutput").ap()

    with ExitStack() as stack:
        c = Ctx(nc, stack)
        CN = c.sb("CN", 128, 456)
        c.dma(CN[:, :], cn_h, writes=[CN])
        I128 = CN[:, 0:128]
        ALL1 = CN[:, 256:384]
        CB = c.sb("CB", 128, 640)
        c.dma(CB[:, :], cb_h, writes=[CB])
        BD, SBD, TWOI = CB[:, 0:128], CB[:, 256:384], CB[:, 512:640]
        wba = [c.sb(f"wba{k}", 128, 16) for k in range(KC)]
        for k in range(KC):
            c.dma(wba[k][:, :], wba_h[k * 128:(k + 1) * 128, :], writes=[wba[k]])
        cwv = [c.sb(f"cwv{k}", 128, 4) for k in range(24)]
        for k in range(24):
            c.dma(cwv[k][:, :], cw_h[k * 128:(k + 1) * 128, :], writes=[cwv[k]])
        rows = c.sb("rows", 1, 160)
        c.dma(rows[:, 0:8], dtb_h.rearrange("(o f) -> o f", o=1), writes=[rows])
        c.dma(rows[:, 8:16], alog_h.rearrange("(o f) -> o f", o=1), writes=[rows])
        c.dma(rows[:, 16:144], nw_h.rearrange("(o f) -> o f", o=1), writes=[rows])
        P = [c.ps(f"pb{i}") for i in range(8)]
        c.act(rows[:, 144:152], rows[:, 8:16], AF.Exp, reads=[rows], writes=[rows])
        c.ts(rows[:, 144:152], rows[:, 144:152], -1.0, OP.mult, reads=[rows], writes=[rows])
        NW = c.sb("NW", 128, 128)
        c.mm(P[0][:, 0:128], ALL1[0:1, :], rows[0:1, 16:144], True, True,
             reads=[CN, rows], writes=[P[0]])
        c.cp(NW[:, :], P[0][:, 0:128], reads=[P[0]], writes=[NW])

        q_tok = c.sb("q_tok", 128, T)
        k_tok = c.sb("k_tok", 128, T)
        vb_tok = c.sb("vb_tok", 128, T)
        xn1T = [c.sb(f"xn1T{k}", 128, T) for k in range(KC)]
        xin = [k_tok[:, 0:1024], k_tok[:, 1024:2048]]
        xnt_ = [vb_tok[:, 0:1024], vb_tok[:, 1024:2048]]
        sqb = [q_tok[:, 0:1024], q_tok[:, 1024:2048]]
        colm = c.sb("colm", 128, 8)
        for t in range(NT):
            r = t % 2
            xt, xnt = xin[r], xnt_[r]
            ts_ = slice(t * 128, (t + 1) * 128)
            c.dma(xt, x1_h[ts_, :], writes=[k_tok])
            c.red(colm[:, 0:1], xt, OP.add, reads=[k_tok], writes=[colm])
            c.ts(colm[:, 0:1], colm[:, 0:1], 1.0 / D, OP.mult, reads=[colm], writes=[colm])
            c.ts(xnt, xt, colm[:, 0:1], OP.subtract, reads=[k_tok, colm], writes=[vb_tok])
            c.act(sqb[r], xnt, AF.Square, reads=[vb_tok], writes=[q_tok])
            c.red(colm[:, 1:2], sqb[r], OP.add, reads=[q_tok], writes=[colm])
            c.ts(colm[:, 1:2], colm[:, 1:2], 1.0 / D, OP.mult, 1e-5, OP.add, reads=[colm], writes=[colm])
            c.act(colm[:, 2:3], colm[:, 1:2], AF.Sqrt, reads=[colm], writes=[colm])
            c.rec(colm[:, 3:4], colm[:, 2:3], reads=[colm], writes=[colm])
            c.ts(xnt, xnt, colm[:, 3:4], OP.mult, reads=[vb_tok, colm], writes=[vb_tok])
            for k in range(KC):
                c.tr(P[k % 2][:, 0:128], xnt[:, k * 128:(k + 1) * 128], I128,
                     reads=[vb_tok, CN], writes=[P[k % 2]])
                c.cp(xn1T[k][:, ts_], P[k % 2][:, 0:128], reads=[P[k % 2]], writes=[xn1T[k]])
        bat = [c.sb(f"bat{t}", 128, 16) for t in range(NT)]
        for t in range(NT):
            ts_ = slice(t * 128, (t + 1) * 128)
            for k in range(KC):
                c.mm(P[2][:, 0:16], xn1T[k][:, ts_], wba[k][:, :], k == 0, k == KC - 1,
                     reads=[xn1T[k], wba[k]], writes=[P[2]])
            c.cp(bat[t][:, :], P[2][:, 0:16], reads=[P[2]], writes=[bat[t]])

        hgT = [c.sb(f"hgT{h}", 128, T, BF16) for h in range(HN)]
        rawT = c.sb("rawT", 128, T + 3)
        strip = c.sb("strip", 128, 512)
        qnT = c.sb("qnT", 128, T)
        knT = c.sb("knT", 128, T)
        zsil = c.sb("zsil", 128, T)
        wzh = [c.sb(f"wzh{r}", 128, 1024) for r in range(2)]
        brow = c.sb("brow", 1, 128)
        glrow = c.sb("glrow", 1, 128)
        bg = c.sb("bg", 128, 64)
        grows = c.sb("grows", 1, T)
        hdr = c.sb("hdr", 128, 2)
        S_sb = c.sb("S_sb", 128, 128)
        Xn = c.sb("Xn", 128, 128)
        Wn = c.sb("Wn", 128, 128)
        Yn = c.sb("Yn", 128, 128)
        dcT = c.sb("dcT", 128, 128)
        triT = c.sb("triT", 128, 128)
        AL = c.sb("AL", 128, 128)
        kbg = c.sb("kbg", 128, 128)
        vbt = c.sb("vbt", 128, 128)
        qe = c.sb("qe", 128, 128)
        qeT = c.sb("qeT", 128, 64)
        kcT = c.sb("kcT", 128, 64)
        vnw = c.sb("vnw", 128, 128)
        vn_s = c.sb("vn_s", 128, 128)
        atn = c.sb("atn", 128, 64)
        hn_s = c.sb("hn_s", 128, 128)
        kdec = c.sb("kdec", 128, 128)
        egl = c.sb("egl", 128, 1)
        ecol = c.sb("ecol", 128, 1)
        scn = c.sb("scn", 128, 128)

        def l2n_inplace(tok, scale_extra, do_T, dstT):
            for t in range(NT):
                ts_ = slice(t * 128, (t + 1) * 128)
                c.act(strip[:, 0:128], tok[:, ts_], AF.Square, reads=[tok], writes=[strip])
                c.red(colm[:, 4:5], strip[:, 0:128], OP.add, reads=[strip], writes=[colm])
                c.ts(colm[:, 4:5], colm[:, 4:5], 1e-6, OP.add, reads=[colm], writes=[colm])
                c.act(colm[:, 5:6], colm[:, 4:5], AF.Sqrt, reads=[colm], writes=[colm])
                c.rec(colm[:, 6:7], colm[:, 5:6], reads=[colm], writes=[colm])
                if scale_extra != 1.0:
                    c.ts(colm[:, 6:7], colm[:, 6:7], scale_extra, OP.mult, reads=[colm], writes=[colm])
                c.ts(tok[:, ts_], tok[:, ts_], colm[:, 6:7], OP.mult, reads=[tok, colm], writes=[tok])
                if do_T:
                    c.tr(P[5][:, 0:128], tok[:, ts_], I128, reads=[tok, CN], writes=[P[5]])
                    c.cp(dstT[:, ts_], P[5][:, 0:128], reads=[P[5]], writes=[dstT])

        for h in range(HN):
            c.dma(wzh[0][:, :].rearrange("p (kc n) -> p kc n", n=128),
                  wqkvz_h[0:1024, 512 * h + 384:512 * h + 512]
                  .rearrange("(kc p) n -> p kc n", p=128), writes=[wzh[0]])
            for g in range(4):
                gs = slice(g * 512, (g + 1) * 512)
                for k in range(KC):
                    c.mm(P[3][:, :], wzh[0][:, k * 128:k * 128 + 128], xn1T[k][:, gs],
                         k == 0, k == KC - 1, reads=[wzh[0], xn1T[k]], writes=[P[3]])
                c.act(zsil[:, gs], P[3][:, :], AF.Silu, reads=[P[3]], writes=[zsil])
            for (ci, coff, tokbuf, dsT) in ((h, 0, q_tok, qnT), (8 + h, 128, k_tok, knT),
                                            (16 + h, 256, vb_tok, None)):
                c.dma(wzh[1][:, :].rearrange("p (kc n) -> p kc n", n=128),
                      wqkvz_h[0:1024, 512 * h + coff:512 * h + coff + 128]
                      .rearrange("(kc p) n -> p kc n", p=128), writes=[wzh[1]])
                c.mset(rawT[:, 0:3], 0.0, writes=[rawT])
                for g in range(4):
                    for k in range(KC):
                        c.mm(P[4][:, :], wzh[1][:, k * 128:k * 128 + 128],
                             xn1T[k][:, g * 512:(g + 1) * 512],
                             k == 0, k == KC - 1, reads=[wzh[1], xn1T[k]], writes=[P[4]])
                    c.cp(rawT[:, 3 + g * 512:3 + (g + 1) * 512], P[4][:, :],
                         reads=[P[4]], writes=[rawT])
                for g in range(4):
                    gs0 = g * 512
                    c.ts(strip[:, :], rawT[:, gs0:gs0 + 512], cwv[ci][:, 0:1], OP.mult,
                         reads=[rawT, cwv[ci]], writes=[strip])
                    for j in (1, 2, 3):
                        c.stt(strip[:, :], rawT[:, gs0 + j:gs0 + j + 512], cwv[ci][:, j:j + 1],
                              strip[:, :], OP.mult, OP.add, reads=[rawT, cwv[ci], strip],
                              writes=[strip])
                    c.act(strip[:, :], strip[:, :], AF.Silu, reads=[strip], writes=[strip])
                    for q4 in range(4):
                        c.tr(P[5][:, 0:128], strip[:, q4 * 128:(q4 + 1) * 128], I128,
                             reads=[strip, CN], writes=[P[5]])
                        c.cp(tokbuf[:, gs0 + q4 * 128:gs0 + (q4 + 1) * 128], P[5][:, 0:128],
                             reads=[P[5]], writes=[tokbuf])
            l2n_inplace(q_tok, float(128 ** -0.5), True, qnT)
            l2n_inplace(k_tok, 1.0, True, knT)
            c.mm(P[6][:, 0:1], ALL1[0:1, :], rows[0:1, h:h + 1], True, True,
                 reads=[CN, rows], writes=[P[6]])
            c.cp(hdr[:, 0:1], P[6][:, 0:1], reads=[P[6]], writes=[hdr])
            c.mm(P[6][:, 0:1], ALL1[0:1, :], rows[0:1, 144 + h:145 + h], True, True,
                 reads=[CN, rows], writes=[P[6]])
            c.cp(hdr[:, 1:2], P[6][:, 0:1], reads=[P[6]], writes=[hdr])
            for t in range(NT):
                c.act(bg[:, 4 * t:4 * t + 1], bat[t][:, 2 * h:2 * h + 1], AF.Sigmoid,
                      reads=[bat[t]], writes=[bg])
                c.tt(colm[:, 4:5], bat[t][:, 2 * h + 1:2 * h + 2], hdr[:, 0:1], OP.add,
                     reads=[bat[t], hdr], writes=[colm])
                c.act(colm[:, 5:6], colm[:, 4:5], AF.Abs, reads=[colm], writes=[colm])
                c.act(colm[:, 5:6], colm[:, 5:6], AF.Exp, scale=-1.0, reads=[colm], writes=[colm])
                c.act(colm[:, 5:6], colm[:, 5:6], AF.Ln, bias=1.0, reads=[colm], writes=[colm])
                c.ts(colm[:, 4:5], colm[:, 4:5], 0.0, OP.max, reads=[colm], writes=[colm])
                c.tt(colm[:, 4:5], colm[:, 4:5], colm[:, 5:6], OP.add, reads=[colm], writes=[colm])
                c.tt(colm[:, 4:5], colm[:, 4:5], hdr[:, 1:2], OP.mult, reads=[colm, hdr], writes=[colm])
                c.mm(P[6][:, 0:1], BD, colm[:, 4:5], True, True, reads=[CB, colm], writes=[P[6]])
                c.cp(bg[:, 4 * t + 1:4 * t + 2], P[6][:, 0:1], reads=[P[6]], writes=[bg])
                c.act(bg[:, 4 * t + 2:4 * t + 3], bg[:, 4 * t + 1:4 * t + 2], AF.Exp,
                      reads=[bg], writes=[bg])
                c.tt(bg[:, 4 * t + 3:4 * t + 4], bg[:, 4 * t:4 * t + 1],
                     bg[:, 4 * t + 2:4 * t + 3], OP.mult, reads=[bg], writes=[bg])
                c.tr(P[6][0:1, 0:128], bg[:, 4 * t + 1:4 * t + 2], I128, reads=[bg, CN], writes=[P[6]])
                c.cp(grows[:, t * 128:(t + 1) * 128], P[6][0:1, 0:128], reads=[P[6]], writes=[grows])
            c.mset(S_sb[:, :], 0.0, writes=[S_sb])
            for t in range(NT):
                ts_ = slice(t * 128, (t + 1) * 128)
                gcol = bg[:, 4 * t + 1:4 * t + 2]
                egc = bg[:, 4 * t + 2:4 * t + 3]
                bcol = bg[:, 4 * t:4 * t + 1]
                becol = bg[:, 4 * t + 3:4 * t + 4]
                c.mm(P[0][:, 0:128], ALL1[0:1, :], grows[:, ts_], True, True,
                     reads=[CN, grows], writes=[P[0]])
                c.ts(dcT[:, :], P[0][:, 0:128], gcol, OP.subtract, reads=[P[0], bg], writes=[dcT])
                c.ts(dcT[:, :], dcT[:, :], 0.0, OP.min, reads=[dcT], writes=[dcT])
                c.act(dcT[:, :], dcT[:, :], AF.Exp, reads=[dcT], writes=[dcT])
                c.tt(dcT[:, :], dcT[:, :], BD, OP.mult, reads=[dcT, CB], writes=[dcT])
                c.tr(P[0][0:1, 0:128], bcol, I128, reads=[bg, CN], writes=[P[0]])
                c.cp(brow[:, :], P[0][0:1, 0:128], reads=[P[0]], writes=[brow])
                c.mm(P[1][:, 0:128], ALL1[0:1, :], brow[:, :], True, True,
                     reads=[CN, brow], writes=[P[1]])
                c.mm(P[2][:, 0:128], knT[:, ts_], knT[:, ts_], True, True,
                     reads=[knT], writes=[P[2]])
                c.tt(triT[:, :], P[2][:, 0:128], dcT[:, :], OP.mult, reads=[P[2], dcT], writes=[triT])
                c.tt(triT[:, :], triT[:, :], P[1][:, 0:128], OP.mult, reads=[triT, P[1]], writes=[triT])
                c.tt(triT[:, :], triT[:, :], SBD, OP.mult, reads=[triT, CB], writes=[triT])
                c.tr(P[2][:, 0:128], triT[:, :], I128, reads=[triT, CN], writes=[P[2]])
                c.tt(AL[:, :], P[2][:, 0:128], I128, OP.add, reads=[P[2], CN], writes=[AL])
                c.tt(Xn[:, :], I128, triT[:, :], OP.subtract, reads=[CN, triT], writes=[Xn])
                c.tt(Wn[:, :], TWOI, AL[:, :], OP.subtract, reads=[CB, AL], writes=[Wn])
                for _ in range(5):
                    c.mm(P[3][:, 0:128], AL[:, :], Xn[:, :], True, True, reads=[AL, Xn], writes=[P[3]])
                    c.tt(Yn[:, :], TWOI, P[3][:, 0:128], OP.subtract, reads=[CB, P[3]], writes=[Yn])
                    c.mm(P[3][:, 0:128], Wn[:, :], Yn[:, :], True, True, reads=[Wn, Yn], writes=[P[3]])
                    c.cp(Xn[:, :], P[3][:, 0:128], reads=[P[3]], writes=[Xn])
                    c.tr(P[3][:, 0:128], Xn[:, :], I128, reads=[Xn, CN], writes=[P[3]])
                    c.cp(Wn[:, :], P[3][:, 0:128], reads=[P[3]], writes=[Wn])
                c.ts(vbt[:, :], vb_tok[:, ts_], bcol, OP.mult, reads=[vb_tok, bg], writes=[vbt])
                c.ts(kbg[:, :], k_tok[:, ts_], becol, OP.mult, reads=[k_tok, bg], writes=[kbg])
                c.ts(qe[:, :], q_tok[:, ts_], egc, OP.mult, reads=[q_tok, bg], writes=[qe])
                for cc in range(2):
                    b0 = 64 * cc
                    sl = slice(b0, b0 + 64)
                    csl = slice(t * 128 + b0, t * 128 + b0 + 64)
                    c.mm(P[4][sl, 0:128], Xn[sl, sl], vbt[sl, :], True, True,
                         reads=[Xn, vbt], writes=[P[4]])
                    c.cp(vnw[sl, :], P[4][sl, 0:128], reads=[P[4]], writes=[vnw])
                    c.mm(P[5][:, 0:64], kbg[sl, :], Xn[sl, sl], True, True,
                         reads=[kbg, Xn], writes=[P[5]])
                    c.cp(kcT[:, :], P[5][:, 0:64], reads=[P[5]], writes=[kcT])
                    c.mm(P[6][sl, 0:128], kcT[:, :], S_sb[:, :], True, True,
                         reads=[kcT, S_sb], writes=[P[6]])
                    c.tt(vn_s[sl, :], vnw[sl, :], P[6][sl, 0:128], OP.subtract,
                         reads=[vnw, P[6]], writes=[vn_s])
                    c.mm(P[7][sl, 0:64], knT[:, csl], qnT[:, csl], True, True,
                         reads=[knT, qnT], writes=[P[7]])
                    c.tt(atn[sl, :], P[7][sl, 0:64], dcT[sl, sl], OP.mult,
                         reads=[P[7], dcT], writes=[atn])
                    c.tr(P[4][:, 0:64], qe[sl, :], I128[sl, sl], reads=[qe, CN], writes=[P[4]])
                    c.cp(qeT[:, :], P[4][:, 0:64], reads=[P[4]], writes=[qeT])
                    c.mm(P[5][sl, 0:128], qeT[:, :], S_sb[:, :], True, False,
                         reads=[qeT, S_sb], writes=[P[5]])
                    c.mm(P[5][sl, 0:128], atn[sl, :], vn_s[sl, :], False, True,
                         reads=[atn, vn_s], writes=[P[5]])
                    c.act(scn[sl, :], P[5][sl, 0:128], AF.Square, reads=[P[5]], writes=[scn])
                    c.red(colm[sl, 4:5], scn[sl, :], OP.add, reads=[scn], writes=[colm])
                    c.ts(colm[sl, 4:5], colm[sl, 4:5], 1.0 / 128.0, OP.mult, 1e-6, OP.add,
                         reads=[colm], writes=[colm])
                    c.act(colm[sl, 5:6], colm[sl, 4:5], AF.Sqrt, reads=[colm], writes=[colm])
                    c.rec(colm[sl, 6:7], colm[sl, 5:6], reads=[colm], writes=[colm])
                    c.ts(hn_s[sl, :], P[5][sl, 0:128], colm[sl, 6:7], OP.mult,
                         reads=[P[5], colm], writes=[hn_s])
                    c.tt(hn_s[sl, :], hn_s[sl, :], NW[sl, :], OP.mult, reads=[hn_s, NW], writes=[hn_s])
                    c.tr(P[6][:, 0:64], hn_s[sl, :], I128[sl, sl], reads=[hn_s, CN], writes=[P[6]])
                    c.tt(hgT[h][:, csl], P[6][:, 0:64], zsil[:, csl], OP.mult,
                         reads=[P[6], zsil], writes=[hgT[h]])
                    gl = grows[:, t * 128 + b0 + 63:t * 128 + b0 + 64]
                    c.mm(P[7][sl, 0:1], ALL1[0:1, 0:64], gl, True, True,
                         reads=[CN, grows], writes=[P[7]])
                    c.tt(ecol[sl, :], P[7][sl, 0:1], bg[sl, 4 * t + 1:4 * t + 2], OP.subtract,
                         reads=[P[7], bg], writes=[ecol])
                    c.act(ecol[sl, :], ecol[sl, :], AF.Exp, reads=[ecol], writes=[ecol])
                    c.ts(kdec[sl, :], k_tok[sl, ts_], ecol[sl, :], OP.mult,
                         reads=[k_tok, ecol], writes=[kdec])
                    c.mm(P[0][:, 0:128], kdec[sl, :], vn_s[sl, :], True, True,
                         reads=[kdec, vn_s], writes=[P[0]])
                    c.mm(P[1][:, 0:1], ALL1[0:1, :], gl, True, True, reads=[CN, grows], writes=[P[1]])
                    c.act(egl[:, :], P[1][:, 0:1], AF.Exp, reads=[P[1]], writes=[egl])
                    c.stt(S_sb[:, :], S_sb[:, :], egl[:, 0:1], P[0][:, 0:128], OP.mult, OP.add,
                          reads=[S_sb, egl, P[0]], writes=[S_sb])
        wob = [c.sb(f"wob{h}", 128, D, BF16) for h in range(HN)]
        wof = c.sb("wof", 128, D)
        outt = [rawT[:, 0:1024], rawT[:, 1024:2048]]
        for h in range(HN):
            c.dma(wof[:, :], wout_h[128 * h:128 * (h + 1), :], writes=[wof])
            c.cp(wob[h][:, :], wof[:, :], reads=[wof], writes=[wob[h]])
        for t in range(NT):
            ts_ = slice(t * 128, (t + 1) * 128)
            ot = outt[t % 2]
            for g in range(2):
                gs = slice(g * 512, (g + 1) * 512)
                for h in range(HN):
                    c.mm(P[2][:, :], hgT[h][:, ts_], wob[h][:, gs], h == 0, h == HN - 1,
                         reads=[hgT[h], wob[h]], writes=[P[2]])
                c.cp(ot[:, gs], P[2][:, :], reads=[P[2]], writes=[rawT])
            c.dma(at_h[ts_, :], ot, reads=[rawT], q="pool")
        c.final_wait("pool")
        c.finalize()
    return nc


def block1_inputs(d, x1):
    cn = host_consts()
    cb = host_consts_b()
    com = dict(
        wqkvz=np.ascontiguousarray(d["L1_attn_W_qkvz"]),
        wba=np.ascontiguousarray(d["L1_attn_W_ba"]),
        cw=np.ascontiguousarray(d["L1_attn_conv_w"].reshape(3072, 4)),
        dtb=np.ascontiguousarray(d["L1_attn_dt_bias"]),
        alog=np.ascontiguousarray(d["L1_attn_A_log"]),
        nw=np.ascontiguousarray(d["L1_attn_norm_weight"]),
        wout=np.ascontiguousarray(d["L1_attn_W_out"]),
        CNH=cn, CBH=cb,
    )
    return [dict(com, x1=np.ascontiguousarray(x1[b])) for b in range(8)]


F16 = mybir.dt.float16


def host_consts_c():
    n, DH, WIN = 2048, 64, 512
    half = np.arange(0, DH, 2, dtype=np.float64)
    inv_freq = 1.0 / 10000.0 ** (half / DH)
    t = np.arange(n, dtype=np.float64)
    fr = t[:, None] * inv_freq[None, :]
    fr = np.concatenate([fr, fr], -1)
    base = (half + 0.4 * DH) / (1.4 * DH)
    power = (t - n // 2) / float(WIN // 2)
    scale = base[None, :] ** power[:, None]
    scale = np.concatenate([scale, scale], -1)
    cos, sin = np.cos(fr), np.sin(fr)
    qcs = (cos * scale * 8.0).T
    qss = (sin * scale * 8.0).T
    kcs = (cos / scale).T
    kss = (sin / scale).T
    rot = np.concatenate([qcs, qss, kcs, kss], axis=1).astype(np.float16)
    il = np.arange(128)[:, None]
    m = np.arange(1408)[None, :]
    keep = ((m - 384) >= il) & ((m - 384) <= il + 512)
    big = np.where(keep, 0.0, -1e30).astype(np.float32)
    RT = np.zeros((128, 64), np.float32)
    for i in range(32):
        RT[i + 32, i] = -1.0
    for i in range(32, 64):
        RT[i - 32, i] = 1.0
    cc2 = np.concatenate([big, RT], axis=1)
    return rot, cc2


def build_l1ffn():
    nc = bass.Bass("TRN2", target_bir_lowering=False)
    T, D, HN, DH, WIN = 2048, 1024, 8, 64, 512
    NT, KC, HID = T // 128, D // 128, 2730
    NW_ = T // WIN
    x1_h = nc.dram_tensor("x1", [T, D], F32, kind="ExternalInput").ap()
    at1_h = nc.dram_tensor("at1", [T, D], F32, kind="ExternalInput").ap()
    wqkv_h = nc.dram_tensor("wqkv", [2, D, 1536], F32, kind="ExternalInput").ap()
    qs_h = nc.dram_tensor("qs", [2, 64], F32, kind="ExternalInput").ap()
    ks_h = nc.dram_tensor("ks", [2, 64], F32, kind="ExternalInput").ap()
    wo_h = nc.dram_tensor("wo", [2, 512, D], F32, kind="ExternalInput").ap()
    lg_h = nc.dram_tensor("lg", [2, D], F32, kind="ExternalInput").ap()
    lb_h = nc.dram_tensor("lb", [2, D], F32, kind="ExternalInput").ap()
    w1_h = nc.dram_tensor("w1", [2, D, 2 * HID], F32, kind="ExternalInput").ap()
    b1_h = nc.dram_tensor("b1", [2, 2 * HID], F32, kind="ExternalInput").ap()
    w2_h = nc.dram_tensor("w2", [2, HID, D], F32, kind="ExternalInput").ap()
    b2_h = nc.dram_tensor("b2", [2, D], F32, kind="ExternalInput").ap()
    cn_h = nc.dram_tensor("CNH", [128, 456], F32, kind="ExternalInput").ap()
    rot_h = nc.dram_tensor("ROTH", [64, 8192], F16, kind="ExternalInput").ap()
    cc2_h = nc.dram_tensor("CC2H", [128, 1472], F32, kind="ExternalInput").ap()
    out_h = nc.dram_tensor("out", [T, D], F32, kind="ExternalOutput").ap()

    rcs = [128] * 21 + [42]
    rco = [sum(rcs[:i]) for i in range(22)]

    with ExitStack() as stack:
        c = Ctx(nc, stack)
        CN = c.sb("CN", 128, 384)
        c.dma(CN[:, :], cn_h[:, 0:384], writes=[CN])
        I128 = CN[:, 0:128]
        ALL1 = CN[:, 256:384]
        ROT = c.sb("ROT", 64, 8192, F16)
        c.dma(ROT[:, :], rot_h, writes=[ROT])
        CC2 = c.sb("CC2", 128, 1472)
        c.dma(CC2[:, :], cc2_h, writes=[CC2])
        BIG = CC2[:, 0:1408]
        RT64 = CC2[:, 1408:1472]
        P = [c.ps(f"pb{i}") for i in range(8)]
        xT = [c.sb(f"xT{k}", 128, T + 1) for k in range(KC)]
        hB = [c.sb(f"hB{k}", 128, T, BF16) for k in range(KC)]
        ABF = c.sb("ABF", 128, 28160, BF16)
        wqb = [ABF[:, k * 1536:(k + 1) * 1536] for k in range(KC)]
        wob_ = [ABF[0:64, 19968 + 1024 * hd:19968 + 1024 * (hd + 1)] for hd in range(HN)]
        w2b = [ABF[:, 1024 * rc:1024 * rc + 1024] for rc in range(22)]
        hhB = ABF[:, 22528:28160]
        wf = c.sb("wf", 128, 1024)
        rows = c.sb("rows", 1, 3200)
        qsc = c.sb("qsc", 64, 2)
        colm = c.sb("colm", 128, 8)
        rrep = c.sb("rrep", 128, 512)
        mrep = None
        qf = c.sb("qf", 64, 512)
        qb_ = ABF[0:64, 16384:16896]
        kcur = [ABF[0:64, 16896:17408], ABF[0:64, 17408:17920]]
        vcur = [ABF[:, 17920:18176], ABF[:, 18176:18432]]
        rr_ = c.sb("rr_", 1, 512)
        sim_s = c.sb("sim_s", 128, 1024)
        q2 = rrep[0:64, :]
        sums = rows[0:1, 2176:2688]
        sums2 = rows[0:1, 2688:3200]
        sq_s = sim_s[:, 0:512]
        mrep = sim_s[:, 512:1024]
        pT = ABF[:, 18432:19456]
        oT_ = ABF[:, 19456:19968]
        w1f = wf
        w1a = c.sb("w1a", 128, 1024, BF16)
        w1g = c.sb("w1g", 128, 1024, BF16)
        b1r = c.sb("b1r", 1, 256)
        osb = wf[:, 0:64]
        kf = wf[0:64, 128:640]
        gbuf = sim_s[:, 0:512]
        vf = sim_s[0:64, 0:512]

        for k in range(KC):
            c.mset(xT[k][:, 0:1], 0.0, writes=[xT[k]])
        for t in range(NT):
            ts_ = slice(t * 128, (t + 1) * 128)
            c.dma(sim_s[:, :], x1_h[ts_, :], writes=[sim_s])
            c.red(colm[:, 0:1], sim_s[:, :], OP.add, reads=[sim_s], writes=[colm])
            c.ts(colm[:, 0:1], colm[:, 0:1], 1.0 / D, OP.mult, reads=[colm], writes=[colm])
            c.ts(sim_s[:, :], sim_s[:, :], colm[:, 0:1], OP.subtract,
                 reads=[sim_s, colm], writes=[sim_s])
            c.act(w1f[:, 0:1024], sim_s[:, :], AF.Square, reads=[sim_s], writes=[wf])
            c.red(colm[:, 1:2], w1f[:, 0:1024], OP.add, reads=[wf], writes=[colm])
            c.ts(colm[:, 1:2], colm[:, 1:2], 1.0 / D, OP.mult, 1e-5, OP.add,
                 reads=[colm], writes=[colm])
            c.act(colm[:, 2:3], colm[:, 1:2], AF.Sqrt, reads=[colm], writes=[colm])
            c.rec(colm[:, 3:4], colm[:, 2:3], reads=[colm], writes=[colm])
            c.ts(sim_s[:, :], sim_s[:, :], colm[:, 3:4], OP.mult, reads=[sim_s, colm], writes=[sim_s])
            for k in range(KC):
                c.tr(P[k % 2][:, 0:128], sim_s[:, k * 128:(k + 1) * 128], I128,
                     reads=[sim_s, CN], writes=[P[k % 2]])
                c.cp(xT[k][:, 1 + t * 128:1 + (t + 1) * 128], P[k % 2][:, 0:128],
                     reads=[P[k % 2]], writes=[xT[k]])
        for l in range(2):
            xs = [xT[k][:, 1:T + 1] if k < 4 else xT[k][:, 0:T] for k in range(KC)]
            for g in range(4):
                gs = slice(g * 512, (g + 1) * 512)
                for k in range(KC):
                    c.cp(hB[k][:, gs], xs[k][:, gs], reads=[xT[k]], writes=[hB[k]])
            for k in range(KC):
                c.dma(wf[:, 0:1024], wqkv_h[l, k * 128:(k + 1) * 128, 0:1024], writes=[wf])
                for g in range(2):
                    c.cp(wqb[k][:, g * 512:(g + 1) * 512], wf[:, g * 512:(g + 1) * 512],
                         reads=[wf], writes=[ABF])
                c.dma(wf[:, 0:512], wqkv_h[l, k * 128:(k + 1) * 128, 1024:1536], writes=[wf])
                c.cp(wqb[k][:, 1024:1536], wf[:, 0:512], reads=[wf], writes=[ABF])
            for hd in range(HN):
                c.dma(wf[0:64, 0:1024], wo_h[l, 64 * hd:64 * (hd + 1), :], writes=[wf])
                c.cp(wob_[hd][:, :], wf[0:64, 0:1024], reads=[wf], writes=[ABF])
            c.dma(rows[:, 2048:2112], qs_h[l, :].rearrange("(o f) -> o f", o=1), writes=[rows])
            c.dma(rows[:, 2112:2176], ks_h[l, :].rearrange("(o f) -> o f", o=1), writes=[rows])
            c.tr(P[0][0:64, 0:1], rows[0:1, 2048:2112], I128[0:1, 0:1], reads=[rows, CN], writes=[P[0]])
            c.cp(qsc[:, 0:1], P[0][0:64, 0:1], reads=[P[0]], writes=[qsc])
            c.tr(P[0][0:64, 0:1], rows[0:1, 2112:2176], I128[0:1, 0:1], reads=[rows, CN], writes=[P[0]])
            c.cp(qsc[:, 1:2], P[0][0:64, 0:1], reads=[P[0]], writes=[qsc])
            for hd in range(HN):
                c.mset(kcur[1][:, :], 0.0, writes=[ABF])
                c.mset(vcur[1][:, :], 0.0, writes=[ABF])
                b0 = 0
                for w in range(NW_):
                    ws = slice(w * WIN, (w + 1) * WIN)
                    cu, pv = w % 2, 1 - (w % 2)
                    for (dst, coff) in ((qf, 64 * hd), (kf, 512 + 64 * hd), (vf, 1024 + 64 * hd)):
                        for k in range(KC):
                            c.mm(P[5][0:64, :], wqb[k][:, coff:coff + 64], hB[k][:, ws],
                                 k == 0, k == KC - 1, reads=[ABF, hB[k]], writes=[P[5]])
                        c.cp(dst[:, :], P[5][0:64, :], reads=[P[5]], writes=[qf if dst is qf else (wf if dst is kf else sim_s)])
                    for (dst, sccol) in ((qf, qsc[:, 0:1]), (kf, qsc[:, 1:2])):
                        dtrk = qf if dst is qf else wf
                        c.act(rrep[0:64, :], dst[:, :], AF.Square, reads=[dtrk], writes=[rrep])
                        c.mm(P[6][0:1, :], ALL1[0:64, 0:1], rrep[0:64, :], True, True,
                             reads=[rrep, CN], writes=[P[6]])
                        c.cp(rr_[:, :], P[6][0:1, :], reads=[P[6]], writes=[rr_])
                        c.act(rr_[:, :], rr_[:, :], AF.Sqrt, reads=[rr_], writes=[rr_])
                        c.ts(rr_[:, :], rr_[:, :], 1e-12, OP.max, reads=[rr_], writes=[rr_])
                        c.rec(rr_[:, :], rr_[:, :], reads=[rr_], writes=[rr_])
                        c.mm(P[6][0:64, :], ALL1[0:1, 0:64], rr_[:, :], True, True,
                             reads=[CN, rr_], writes=[P[6]])
                        c.tt(dst[:, :], dst[:, :], P[6][0:64, :], OP.mult,
                             reads=[dtrk, P[6]], writes=[dtrk])
                        c.ts(dst[:, :], dst[:, :], sccol, OP.mult, reads=[dtrk, qsc], writes=[dtrk])
                    for (dst, c0, c1) in ((qf, 0, 2048), (kf, 4096, 6144)):
                        dtrk = qf if dst is qf else wf
                        c.mm(P[7][0:64, :], RT64[0:64, :], dst[:, :], True, True,
                             reads=[CC2, dtrk], writes=[P[7]])
                        c.tt(q2[:, :], dst[:, :], ROT[:, c0 + w * WIN:c0 + (w + 1) * WIN],
                             OP.mult, reads=[dtrk, ROT], writes=[rrep])
                        c.tt(dst[:, :], P[7][0:64, :], ROT[:, c1 + w * WIN:c1 + (w + 1) * WIN],
                             OP.mult, reads=[P[7], ROT], writes=[dtrk])
                        c.tt(dst[:, :], dst[:, :], q2[:, :], OP.add, reads=[dtrk, rrep], writes=[dtrk])
                    c.cp(qb_[:, :], qf[:, :], reads=[qf], writes=[ABF])
                    c.cp(kcur[cu][:, :], kf[:, :], reads=[wf], writes=[ABF])
                    for q4 in range(4):
                        c.tr(P[7][:, 0:64], vf[:, q4 * 128:(q4 + 1) * 128], I128[0:64, 0:64],
                             reads=[sim_s, CN], writes=[P[7]])
                        c.cp(vcur[cu][:, q4 * 64:(q4 + 1) * 64], P[7][:, 0:64],
                             reads=[P[7]], writes=[ABF])
                    for r in range(4):
                        isl = slice(r * 128, (r + 1) * 128)
                        c.mm(P[0][:, :], qb_[:, isl], kcur[pv][:, :], True, True,
                             reads=[ABF], writes=[P[0]])
                        c.mm(P[1][:, :], qb_[:, isl], kcur[cu][:, :], True, True,
                             reads=[ABF], writes=[P[1]])
                        c.tt(sim_s[:, 0:512], P[0][:, :], BIG[:, 384 - 128 * r:896 - 128 * r],
                             OP.add, reads=[P[0], CC2], writes=[sim_s])
                        c.tt(sim_s[:, 512:1024], P[1][:, :], BIG[:, 896 - 128 * r:1408 - 128 * r],
                             OP.add, reads=[P[1], CC2], writes=[sim_s])
                        if w == 0:
                            c.ts(sim_s[:, 0:512], sim_s[:, 0:512], -1e30, OP.add,
                                 reads=[sim_s], writes=[sim_s])
                        c.red(colm[:, 0:1], sim_s[:, :], OP.max, reads=[sim_s], writes=[colm])
                        c.ts(colm[:, 1:2], colm[:, 0:1], -1.0, OP.mult, reads=[colm], writes=[colm])
                        c.act(sim_s[:, :], sim_s[:, :], AF.Exp, bias=colm[:, 1:2],
                              reads=[sim_s, colm], writes=[sim_s])
                        c.red(colm[:, 2:3], sim_s[:, :], OP.add, reads=[sim_s], writes=[colm])
                        c.rec(colm[:, 3:4], colm[:, 2:3], reads=[colm], writes=[colm])
                        c.ts(sim_s[:, :], sim_s[:, :], colm[:, 3:4], OP.mult,
                             reads=[sim_s, colm], writes=[sim_s])
                        for j in range(8):
                            c.tr(P[2][:, 0:128], sim_s[:, j * 128:(j + 1) * 128], I128,
                                 reads=[sim_s, CN], writes=[P[2]])
                            c.cp(pT[:, j * 128:(j + 1) * 128], P[2][:, 0:128],
                                 reads=[P[2]], writes=[ABF])
                        for j in range(8):
                            vsrc = vcur[pv] if j < 4 else vcur[cu]
                            j4 = j % 4
                            c.mm(P[3][:, 0:64], pT[:, j * 128:(j + 1) * 128],
                                 vsrc[:, j4 * 64:(j4 + 1) * 64], j == 0, j == 7,
                                 reads=[ABF], writes=[P[3]])
                        c.cp(osb[:, :], P[3][:, 0:64], reads=[P[3]], writes=[wf])
                        c.tr(P[4][0:64, 0:128], osb[:, :], I128, reads=[wf, CN], writes=[P[4]])
                        c.cp(oT_[0:64, isl], P[4][0:64, 0:128],
                             reads=[P[4]], writes=[ABF])
                    for dc in range(KC):
                        c.mm(P[5][:, :], wob_[hd][:, dc * 128:(dc + 1) * 128],
                             oT_[0:64, :], True, True,
                             reads=[ABF], writes=[P[5]])
                        c.tt(xT[dc][:, 1 + w * WIN:1 + (w + 1) * WIN],
                             xT[dc][:, 1 + w * WIN:1 + (w + 1) * WIN], P[5][:, :], OP.add,
                             reads=[xT[dc], P[5]], writes=[xT[dc]])
            c.dma(rows[:, 0:1024], lg_h[l, :].rearrange("(o f) -> o f", o=1), writes=[rows])
            c.dma(rows[:, 1024:2048], lb_h[l, :].rearrange("(o f) -> o f", o=1), writes=[rows])
            xs = [xT[k][:, 1:T + 1] if k < 4 else xT[k][:, 0:T] for k in range(KC)]
            for g in range(4):
                gs = slice(g * 512, (g + 1) * 512)
                for k in range(KC):
                    c.mm(P[0][0:1, :], ALL1[:, 0:1], xs[k][:, gs], k == 0, k == KC - 1,
                         reads=[xT[k], CN], writes=[P[0]])
                c.ts(sums[:, :], P[0][0:1, :], 1.0 / D, OP.mult, reads=[P[0]], writes=[rows])
                for k in range(KC):
                    c.act(sq_s[:, :], xs[k][:, gs], AF.Square, reads=[xT[k]], writes=[sim_s])
                    c.mm(P[1][0:1, :], ALL1[:, 0:1], sq_s[:, :], k == 0, k == KC - 1,
                         reads=[sim_s, CN], writes=[P[1]])
                c.ts(sums2[:, :], P[1][0:1, :], 1.0 / D, OP.mult, reads=[P[1]], writes=[rows])
                c.act(rr_[:, :], sums[:, :], AF.Square, reads=[rows], writes=[rr_])
                c.tt(sums2[:, :], sums2[:, :], rr_[:, :], OP.subtract,
                     reads=[rows, rr_], writes=[rows])
                c.ts(sums2[:, :], sums2[:, :], 1e-5, OP.add, reads=[rows], writes=[rows])
                c.act(sums2[:, :], sums2[:, :], AF.Sqrt, reads=[rows], writes=[rows])
                c.rec(sums2[:, :], sums2[:, :], reads=[rows], writes=[rows])
                c.mm(P[2][:, :], ALL1[0:1, :], sums[0:1, :], True, True,
                     reads=[CN, rows], writes=[P[2]])
                c.cp(mrep[:, :], P[2][:, :], reads=[P[2]], writes=[sim_s])
                c.mm(P[3][:, :], ALL1[0:1, :], sums2[0:1, :], True, True,
                     reads=[CN, rows], writes=[P[3]])
                c.cp(rrep[:, :], P[3][:, :], reads=[P[3]], writes=[rrep])
                for k in range(KC):
                    c.tr(P[4][:, 0:1], rows[0:1, k * 128:(k + 1) * 128], I128[0:1, 0:1],
                         reads=[rows, CN], writes=[P[4]])
                    c.cp(colm[:, 4:5], P[4][:, 0:1], reads=[P[4]], writes=[colm])
                    c.tr(P[4][:, 0:1], rows[0:1, 1024 + k * 128:1024 + (k + 1) * 128],
                         I128[0:1, 0:1], reads=[rows, CN], writes=[P[4]])
                    c.cp(colm[:, 5:6], P[4][:, 0:1], reads=[P[4]], writes=[colm])
                    c.tt(sq_s[:, :], xs[k][:, gs], mrep[:, :], OP.subtract,
                         reads=[xT[k], sim_s], writes=[sim_s])
                    c.tt(sq_s[:, :], sq_s[:, :], rrep[:, :], OP.mult,
                         reads=[sim_s, rrep], writes=[sim_s])
                    c.ts(hB[k][:, gs], sq_s[:, :], colm[:, 4:5], OP.mult, colm[:, 5:6], OP.add,
                         reads=[sim_s, colm], writes=[hB[k]])
            for rc in range(22):
                nr = rcs[rc]
                c.dma(wf[0:nr, 0:1024], w2_h[l, rco[rc]:rco[rc] + nr, :], writes=[wf])
                for g in range(2):
                    c.cp(w2b[rc][0:nr, g * 512:(g + 1) * 512], wf[0:nr, g * 512:(g + 1) * 512],
                         reads=[wf], writes=[ABF])
            c.dma(rows[:, 2176:3200], b2_h[l, :].rearrange("(o f) -> o f", o=1), writes=[rows])
            for tg in range(8):
                gs = slice(tg * 256, (tg + 1) * 256)
                for rc in range(22):
                    nr = rcs[rc]
                    c.dma(w1f[:, 0:nr * 8].rearrange("p (kc n) -> p kc n", n=nr),
                          w1_h[l, :, rco[rc]:rco[rc] + nr]
                          .rearrange("(kc p) n -> p kc n", p=128), writes=[wf])
                    for k in range(KC):
                        c.cp(w1a[:, k * nr:(k + 1) * nr], w1f[:, k * nr:(k + 1) * nr],
                             reads=[wf], writes=[w1a])
                    c.dma(w1f[:, 0:nr * 8].rearrange("p (kc n) -> p kc n", n=nr),
                          w1_h[l, :, HID + rco[rc]:HID + rco[rc] + nr]
                          .rearrange("(kc p) n -> p kc n", p=128), writes=[wf])
                    for k in range(KC):
                        c.cp(w1g[:, k * nr:(k + 1) * nr], w1f[:, k * nr:(k + 1) * nr],
                             reads=[wf], writes=[w1g])
                    c.dma(b1r[:, 0:nr], b1_h[l, rco[rc]:rco[rc] + nr]
                          .rearrange("(o f) -> o f", o=1), writes=[b1r])
                    c.dma(b1r[:, 128:128 + nr], b1_h[l, HID + rco[rc]:HID + rco[rc] + nr]
                          .rearrange("(o f) -> o f", o=1), writes=[b1r])
                    c.tr(P[3][0:nr, 0:1], b1r[0:1, 0:nr], I128[0:1, 0:1], reads=[b1r, CN], writes=[P[3]])
                    c.cp(colm[0:nr, 6:7], P[3][0:nr, 0:1], reads=[P[3]], writes=[colm])
                    c.tr(P[3][0:nr, 0:1], b1r[0:1, 128:128 + nr], I128[0:1, 0:1],
                         reads=[b1r, CN], writes=[P[3]])
                    c.cp(colm[0:nr, 7:8], P[3][0:nr, 0:1], reads=[P[3]], writes=[colm])
                    for k in range(KC):
                        c.mm(P[0][0:nr, 0:256], w1a[:, k * nr:(k + 1) * nr], hB[k][:, gs],
                             k == 0, k == KC - 1, reads=[w1a, hB[k]], writes=[P[0]])
                    for k in range(KC):
                        c.mm(P[1][0:nr, 0:256], w1g[:, k * nr:(k + 1) * nr], hB[k][:, gs],
                             k == 0, k == KC - 1, reads=[w1g, hB[k]], writes=[P[1]])
                    c.act(gbuf[0:nr, 0:256], P[1][0:nr, 0:256], AF.Gelu, bias=colm[0:nr, 7:8],
                          reads=[P[1], colm], writes=[sim_s])
                    c.stt(hhB[0:nr, rc * 256:(rc + 1) * 256], P[0][0:nr, 0:256], colm[0:nr, 6:7],
                          gbuf[0:nr, 0:256], OP.add, OP.mult, reads=[P[0], colm, sim_s], writes=[ABF])
                for dc in range(KC):
                    for rc in range(22):
                        nr = rcs[rc]
                        c.mm(P[2][:, 0:256], w2b[rc][0:nr, dc * 128:(dc + 1) * 128],
                             hhB[0:nr, rc * 256:(rc + 1) * 256], rc == 0, rc == 21,
                             reads=[ABF], writes=[P[2]])
                    c.tr(P[3][:, 0:1], rows[0:1, 2176 + dc * 128:2176 + (dc + 1) * 128],
                         I128[0:1, 0:1], reads=[rows, CN], writes=[P[3]])
                    c.cp(colm[:, 6:7], P[3][:, 0:1], reads=[P[3]], writes=[colm])
                    c.stt(xT[dc][:, 1 + tg * 256:1 + (tg + 1) * 256], P[2][:, 0:256],
                          colm[:, 6:7], xT[dc][:, 1 + tg * 256:1 + (tg + 1) * 256],
                          OP.add, OP.add, reads=[P[2], colm, xT[dc]], writes=[xT[dc]])
        for t in range(NT):
            ts_ = slice(t * 128, (t + 1) * 128)
            c.dma(w1f[:, 0:1024], x1_h[ts_, :], writes=[wf])
            c.dma(rrep[:, :], at1_h[ts_, 0:512], writes=[rrep])
            for k in range(KC):
                c.tr(P[k % 2][:, 0:128], xT[k][:, 1 + t * 128:1 + (t + 1) * 128], I128,
                     reads=[xT[k], CN], writes=[P[k % 2]])
                c.cp(sim_s[:, k * 128:(k + 1) * 128], P[k % 2][:, 0:128],
                     reads=[P[k % 2]], writes=[sim_s])
            c.tt(sim_s[:, :], sim_s[:, :], w1f[:, 0:1024], OP.add, reads=[sim_s, wf], writes=[sim_s])
            c.tt(sim_s[:, 0:512], sim_s[:, 0:512], rrep[:, :], OP.add,
                 reads=[sim_s, rrep], writes=[sim_s])
            c.dma(rrep[:, :], at1_h[ts_, 512:1024], writes=[rrep])
            c.tt(sim_s[:, 512:1024], sim_s[:, 512:1024], rrep[:, :], OP.add,
                 reads=[sim_s, rrep], writes=[sim_s])
            c.dma(out_h[ts_, :], sim_s[:, :], reads=[sim_s], q="pool")
        c.final_wait("pool")
        c.finalize()
    return nc


def l1ffn_inputs(d, x1, at1):
    cn = host_consts()
    rot, cc2 = host_consts_c()
    com = dict(
        wqkv=np.ascontiguousarray(d["L1_ffn_wqkv"]),
        qs=np.ascontiguousarray(d["L1_ffn_q_scale"]),
        ks=np.ascontiguousarray(d["L1_ffn_k_scale"]),
        wo=np.ascontiguousarray(d["L1_ffn_wo"]),
        lg=np.ascontiguousarray(d["L1_ffn_ln_g"]),
        lb=np.ascontiguousarray(d["L1_ffn_ln_b"]),
        w1=np.ascontiguousarray(d["L1_ffn_w1"]),
        b1=np.ascontiguousarray(d["L1_ffn_b1"]),
        w2=np.ascontiguousarray(d["L1_ffn_w2"]),
        b2=np.ascontiguousarray(d["L1_ffn_b2"]),
        CNH=cn, ROTH=rot, CC2H=cc2,
    )
    return [dict(com, x1=np.ascontiguousarray(x1[b]), at1=np.ascontiguousarray(at1[b]))
            for b in range(8)]


def kernel(**inputs):
    d = inputs
    nc_a = build_block0()
    res_a = run_bass_kernel_spmd(nc_a, block0_inputs(d), core_ids=list(range(8)))
    x1 = np.stack([np.asarray(r["x1"]) for r in res_a.results])
    nc_b = build_l1attn()
    res_b = run_bass_kernel_spmd(nc_b, block1_inputs(d, x1), core_ids=list(range(8)))
    at1 = np.stack([np.asarray(r["attn1"]) for r in res_b.results])
    nc_c = build_l1ffn()
    res_c = run_bass_kernel_spmd(nc_c, l1ffn_inputs(d, x1, at1), core_ids=list(range(8)))
    return np.stack([np.asarray(r["out"]) for r in res_c.results])
